# version 13 (final)
# speedup vs baseline: 1.0197x; 1.0005x over previous
import math
import numpy as np
from contextlib import ExitStack

import concourse.bass as bass
import concourse.mybir as mybir
from concourse.bass import IndirectOffsetOnAxis
from concourse.bass_utils import run_bass_kernel_spmd

F32 = mybir.dt.float32
BF16 = mybir.dt.bfloat16
I32 = mybir.dt.int32
ALU = mybir.AluOpType
AF = mybir.ActivationFunctionType
AX = mybir.AxisListType.X

S = 2048
HD = 1024
E = 32
C = 640
EPC = 4
NT = 16
NCH = 8
NCORE = 8
CB = 5
ALPHA = 0.99
LNA = math.log(ALPHA)
EPS = 1e-5
SCL = 1.0 / 32.0
BIG = 1 << 20

COMPUTE_ENGS = ("tensor", "vector", "scalar", "gpsimd")
DMA_SEMS = ("LD0", "LD1", "LD2", "LD3", "ST0", "ST1", "ST2", "ST3", "WQ0", "WQ1", "WV", "SCT", "GA0", "GA1", "MISC", "CNT", "YST0", "YST1")


class KB:
    def __init__(self):
        self.nc = bass.Bass("TRN2", target_bir_lowering=False)
        nc = self.nc
        self.x_in = nc.dram_tensor("x", [S, HD], F32, kind="ExternalInput")
        self.wqkvT = nc.dram_tensor("wqkvT", [HD, 3 * HD], F32, kind="ExternalInput")
        self.bqkvT_in = nc.dram_tensor("bqkvT", [128, 24], F32, kind="ExternalInput")
        self.bqkv_in = nc.dram_tensor("bqkv", [3 * HD], F32, kind="ExternalInput")
        self.wrouter_in = nc.dram_tensor("wrouter", [HD, E], F32, kind="ExternalInput")
        self.wg_in = nc.dram_tensor("wg", [EPC, HD, HD], F32, kind="ExternalInput")
        self.wu_in = nc.dram_tensor("wu", [EPC, HD, HD], F32, kind="ExternalInput")
        self.wd_in = nc.dram_tensor("wd", [EPC, HD, HD], F32, kind="ExternalInput")
        self.esel_in = nc.dram_tensor("esel", [E, EPC], F32, kind="ExternalInput")
        self.pmask_in = nc.dram_tensor("pmask", [NCORE], F32, kind="ExternalInput")
        self.out_t = nc.dram_tensor("out", [S, HD], F32, kind="ExternalOutput")
        self.xa = nc.dram_tensor("xa", [S, HD], F32)
        self.xb = nc.dram_tensor("xb", [S, HD], F32)
        self.cc_cnt_in = [nc.dram_tensor(f"cnt_in{i}", [E], F32) for i in range(2)]
        self.cc_cnt_out = [nc.dram_tensor(f"cnt_out{i}", [NCORE * E], F32, addr_space="Shared") for i in range(2)]
        self.cc_dsp_in = [nc.dram_tensor(f"dsp_in{i}", [S, HD], BF16) for i in range(2)]
        self.cc_dsp_out = [nc.dram_tensor(f"dsp_out{i}", [NCORE * S, HD], BF16, addr_space="Shared") for i in range(2)]
        self.cc_y_in = [[nc.dram_tensor(f"y_in{i}{w}", [2 * C, HD], BF16) for w in range(2)]
                        for i in range(2)]
        self.cc_y_out = [[nc.dram_tensor(f"y_out{i}{w}", [NCORE * 2 * C, HD], BF16, addr_space="Shared")
                          for w in range(2)] for i in range(2)]

        self.cnt = {}
        self.waited = {}
        self.last = {}

    def emit(self, engname, fn):
        getattr(self.block, engname)(fn)

    def w(self, engname, eng, *toks):
        for t in toks:
            if t is None:
                continue
            nm, v = t
            if v > self.waited.get((engname, nm), 0):
                eng.wait_ge(self.sem[nm], v)
                self.waited[(engname, nm)] = v

    def rec(self, engname, inst):
        self.last[engname] = [inst, None]
        return inst

    ENGOBJ = {"tensor": "tensor", "vector": "vector", "scalar": "scalar", "gpsimd": "gpsimd"}

    def tok(self, engname, drain=False):
        pair = self.last.get(engname)
        if pair is None:
            return None
        if pair[1] is None:
            self.cnt[engname] = self.cnt.get(engname, 0) + 1
            if drain:
                sem = self.sem[engname]

                def f(eng, sem=sem):
                    eng.drain().then_inc(sem, 1)
                self.emit(engname, f)
            else:
                pair[0].then_inc(self.sem[engname], 1)
            pair[1] = (engname, self.cnt[engname])
        return pair[1]

    def dtok(self, qname, inst):
        self.cnt[qname] = self.cnt.get(qname, 0) + 16
        inst.then_inc(self.sem[qname], 16)
        return (qname, self.cnt[qname])

    def cctok(self, inst):
        self.cnt["CC"] = self.cnt.get("CC", 0) + 1
        inst.then_inc(self.sem["CC"], 1)
        return ("CC", self.cnt["CC"])

    def barrier(self, cc=True):
        toks = [self.tok(e, drain=True) for e in COMPUTE_ENGS]
        qs = DMA_SEMS + (("CC",) if cc else ())
        for q in qs:
            if self.cnt.get(q, 0):
                toks.append((q, self.cnt[q]))
        toks = [t for t in toks if t]
        for engname in COMPUTE_ENGS + ("sync",):
            def f(eng, engname=engname, toks=tuple(toks)):
                self.w(engname, eng, *toks)
            self.emit(engname, f)

    def dump(self, name, ap):
        if name not in self._dumps:
            return
        shp = list(ap.shape) if hasattr(ap, "shape") else None
        dten = self.nc.dram_tensor("dbg_" + name, shp, ap.dtype, kind="ExternalOutput")

        def f(eng, dten=dten, ap=ap):
            self.dtok("ST0", eng.dma_start(out=dten.ap(), in_=ap))
        self.emit("sync", f)

    def build(self, parts=("ffn0", "attn", "ffn1"), ffn_stop=None, dumps=()):
        self._dumps = set(dumps)
        self._ffn_stop = ffn_stop
        nc = self.nc
        with ExitStack() as top:
            self.sem = {}
            for n in ("T", "V", "Sc", "G", "CC") + DMA_SEMS:
                self.sem[n] = top.enter_context(nc.semaphore(n))
            self.sem["tensor"] = self.sem["T"]
            self.sem["vector"] = self.sem["V"]
            self.sem["scalar"] = self.sem["Sc"]
            self.sem["gpsimd"] = self.sem["G"]
            self.block = top.enter_context(nc.Block())

            self.idn = top.enter_context(nc.sbuf_tensor("idn", [128, 128], F32))
            self.idnb = top.enter_context(nc.sbuf_tensor("idnb", [128, 128], BF16))
            self.iotaE = top.enter_context(nc.sbuf_tensor("iotaE", [128, E], F32))
            self.iotaEm = top.enter_context(nc.sbuf_tensor("iotaEm", [128, E], F32))
            self.iotc = top.enter_context(nc.sbuf_tensor("iotc", [128, 1], F32))
            self.iotr = top.enter_context(nc.sbuf_tensor("iotr", [128, 128], F32))
            self.ones1 = top.enter_context(nc.sbuf_tensor("ones1", [1, 128], F32))
            self.wr_sb = top.enter_context(nc.sbuf_tensor("wr_sb", [128, NCH, E], F32))
            self.eselsb = top.enter_context(nc.sbuf_tensor("eselsb", [E, EPC], F32))
            self.pmrow = top.enter_context(nc.sbuf_tensor("pmrow", [1, NCORE], F32))
            self.Lsb = top.enter_context(nc.sbuf_tensor("Lsb", [E, E], F32))
            self.csts = top.enter_context(nc.sbuf_tensor("csts", [128, 4], F32))
            self.hs_sum = [top.enter_context(nc.sbuf_tensor(f"hs_sum{k}", [128, NT], F32)) for k in range(2)]
            self.hs_ssq = [top.enter_context(nc.sbuf_tensor(f"hs_ssq{k}", [128, NT], F32)) for k in range(2)]
            self.hs_tok = [None, None]

            self.setup()
            if "ffn0" in parts:
                self.ffn(0, self.x_in, self.xa, stats_out=0 if "attn" in parts else None)
            if "attn" in parts:
                self.attn(stats_in=0 if "ffn0" in parts else None, stats_out=1 if "ffn1" in parts else None)
            if "ffn1" in parts:
                self.ffn(1, self.xb, self.out_t, stats_in=1 if "attn" in parts else None)
            self.barrier()
            for nm, dt_ in (("xa", self.xa), ("xb", self.xb)):
                if nm in self._dumps:
                    dten = self.nc.dram_tensor("dbg_" + nm, [S, HD], F32, kind="ExternalOutput")

                    def fcp(eng, dten=dten, dt_=dt_):
                        self.dtok("ST0", eng.dma_start(out=dten[:, :], in_=dt_[:, :]))
                    self.emit("sync", fcp)
            self.barrier()

            def fin(eng):
                self.w("sync", eng, ("ST0", self.cnt.get("ST0", 0)), ("ST1", self.cnt.get("ST1", 0)))
            self.emit("sync", fin)
        return nc

    def setup(self):
        def g_work(g):
            self.rec("gpsimd", g.iota(self.iotr[:, :], [[1, 128]], base=0, channel_multiplier=0,
                                      allow_small_or_imprecise_dtypes=True))
            self.rec("gpsimd", g.iota(self.iotc[:, :], [[0, 1]], base=0, channel_multiplier=1,
                                      allow_small_or_imprecise_dtypes=True))
            self.rec("gpsimd", g.iota(self.iotaE[:, :], [[1, E]], base=0, channel_multiplier=0,
                                      allow_small_or_imprecise_dtypes=True))
            self.rec("gpsimd", g.iota(self.iotaEm[:, :], [[1, E]], base=-BIG, channel_multiplier=0,
                                      allow_small_or_imprecise_dtypes=True))
            self.dtok("MISC", g.dma_start(out=self.wr_sb[:, :, :],
                                          in_=self.wrouter_in.ap().rearrange("(c p) e -> p c e", p=128)))
            self.dtok("MISC", g.dma_start(out=self.eselsb[:, :], in_=self.esel_in[:, :]))
            self.dtok("MISC", g.dma_start(out=self.pmrow[0:1, :],
                                          in_=self.pmask_in.ap().rearrange("(one j) -> one j", one=1)))
        self.emit("gpsimd", g_work)
        gt = self.tok("gpsimd")

        def v_work(v):
            self.w("vector", v, gt, ("MISC", self.cnt["MISC"]))
            self.rec("vector", v.tensor_scalar(self.idn[:, :], self.iotr[:, :], self.iotc[:, 0:1],
                                               None, ALU.is_equal))
            v.drain()
            self.rec("vector", v.tensor_copy(self.idnb[:, :], self.idn[:, :]))
            self.rec("vector", v.memset(self.ones1[0:1, :], 1.0))
            self.rec("vector", v.tensor_scalar(self.Lsb[:, :], self.iotr[0:E, 0:E], self.iotc[0:E, 0:1],
                                               None, ALU.is_gt))
            self.rec("vector", v.memset(self.csts[:, 0:1], EPS))
            for r in range(1, 4):
                self.rec("vector", v.memset(self.csts[:, r:r + 1], -128.0 * r * LNA))
        self.emit("vector", v_work)
        self.barrier()

    def ffn(self, inv, xsrc, xdst, stats_in=None, stats_out=None):
        nc = self.nc
        cnt_in, cnt_out = self.cc_cnt_in[inv], self.cc_cnt_out[inv]
        dsp_in, dsp_out = self.cc_dsp_in[inv], self.cc_dsp_out[inv]
        y_in, y_out = self.cc_y_in[inv], self.cc_y_out[inv]
        with ExitStack() as inv_es:
            def it(name, shape, dt=F32):
                return inv_es.enter_context(nc.sbuf_tensor(f"i{inv}_{name}", shape, dt))
            mcol = it("mcol", [128, NT]); rcol = it("rcol", [128, NT]); negmr = it("negmr", [128, NT])
            top1c = it("top1c", [128, NT]); gatec = it("gatec", [128, NT]); poslc = it("poslc", [128, NT])
            geff = it("geff", [128, NT]); geffB = it("geffB", [128, NT])
            reti = it("reti", [128, NT], I32)
            oh_em = it("oh_em", [E, S]); cums = it("cums", [E, S])
            basebc = it("basebc", [128, E]); dltbc = it("dltbc", [128, E])
            baseownbc = it("baseownbc", [128, E])

            with ExitStack() as es:
                def tt(name, shape, dt=F32):
                    return es.enter_context(nc.sbuf_tensor(f"i{inv}a_{name}", shape, dt))
                xt = tt("xt", [128, NT, HD])
                zt = tt("zt", [128, 2, HD])
                zbf = tt("zbf", [128, NT, HD], BF16)
                ssum = tt("ssum", [128, NT]); ssq = tt("ssq", [128, NT])
                sqs = tt("sqs", [128, HD])
                zfm = tt("zfm", [128, 2, HD])
                logits = tt("logits", [128, NT, E])
                logT = tt("logT", [E, 2, 128])
                expsc = tt("expsc", [128, E])
                mx8 = tt("mx8", [128, 8]); nmx = tt("nmx", [128, 2])
                varc = tt("varc", [128, NT])
                eqf = tt("eqf", [128, E]); cand = tt("cand", [128, E]); mn = tt("mn", [128, 1])
                den = tt("den", [128, NT])
                ohtm = tt("ohtm", [128, E]); prod = tt("prod", [128, E])
                ptr = [es.enter_context(nc.psum_tensor(f"i{inv}_ptr{k}", [128, 512], F32)) for k in range(4)]
                plog = es.enter_context(nc.psum_tensor(f"i{inv}_plog", [E, 128], F32))
                pl2 = [es.enter_context(nc.psum_tensor(f"i{inv}_pl2{k}", [128, E], F32)) for k in range(2)]
                pmisc = es.enter_context(nc.psum_tensor(f"i{inv}_pmisc", [128, 512], F32))

                ld_toks = {}
                stats_done = {}
                if stats_in is not None:
                    for ti in range(NT):
                        def ld(eng, ti=ti):
                            if ti >= 4:
                                self.w("sync", eng, ld_toks[ti - 4])
                            ld_toks[ti] = self.dtok("LD%d" % (ti % 4),
                                                    eng.dma_start(out=xt[:, ti, :],
                                                                  in_=xsrc[ti * 128:(ti + 1) * 128, :]))
                        self.emit("sync", ld)
                        stats_done[ti] = (ld_toks[ti],)
                for ti in (() if stats_in is not None else range(NT)):
                    par = ti % 2

                    def ld(eng, ti=ti):
                        if ti >= 2:
                            self.w("sync", eng, *stats_done[ti - 2])
                        ld_toks[ti] = self.dtok("LD%d" % (ti % 2),
                                                eng.dma_start(out=xt[:, ti, :],
                                                              in_=xsrc[ti * 128:(ti + 1) * 128, :]))
                    self.emit("sync", ld)

                    def vst(v, ti=ti):
                        self.w("vector", v, ld_toks[ti])
                        self.rec("vector", v.reduce_sum(ssum[:, ti:ti + 1], xt[:, ti, :], AX))
                    self.emit("vector", vst)
                    v_tok = self.tok("vector")

                    def sst(s, ti=ti):
                        self.w("scalar", s, ld_toks[ti])
                        if ti >= 1:
                            self.w("scalar", s, stats_done[ti - 1][1])
                        self.rec("scalar", s.activation(sqs[:, :], xt[:, ti, :], AF.Square,
                                                        accum_out=ssq[:, ti:ti + 1]))
                    self.emit("scalar", sst)
                    stats_done[ti] = (v_tok, self.tok("scalar"))

                if stats_in is not None:
                    ssum = self.hs_sum[stats_in]
                    ssq = self.hs_ssq[stats_in]

                def vcols(v):
                    if stats_in is None:
                        self.w("vector", v, *stats_done[NT - 1])
                        v.drain()
                    else:
                        self.w("vector", v, self.hs_tok[stats_in])
                    self.rec("vector", v.tensor_scalar(mcol[:, :], ssum[:, :], 1.0 / HD, None, ALU.mult))
                    v.drain()
                    self.rec("vector", v.tensor_tensor(varc[:, :], mcol[:, :], mcol[:, :], ALU.mult))
                    v.drain()
                    self.rec("vector", v.tensor_scalar(varc[:, :], varc[:, :], -1.0, None, ALU.mult))
                    v.drain()
                    self.rec("vector", v.scalar_tensor_tensor(varc[:, :], ssq[:, :], 1.0 / HD, varc[:, :],
                                                              ALU.mult, ALU.add))
                self.emit("vector", vcols)
                vt = self.tok("vector")

                def scols(s):
                    self.w("scalar", s, vt)
                    self.rec("scalar", s.activation(rcol[:, :], varc[:, :], AF.Sqrt, bias=self.csts[:, 0:1], scale=1.0))
                self.emit("scalar", scols)
                st = self.tok("scalar")

                def vcols2(v):
                    self.w("vector", v, st)
                    self.rec("vector", v.reciprocal(rcol[:, :], rcol[:, :]))
                    v.drain()
                    self.rec("vector", v.scalar_tensor_tensor(negmr[:, :], mcol[:, :], -1.0, rcol[:, :],
                                                              ALU.mult, ALU.mult))
                    v.drain()
                self.emit("vector", vcols2)
                nrm_tok = self.tok("vector")

                if self._ffn_stop == "passA":
                    self.barrier()
                    if inv == 0:
                        self.dump("ssum", ssum[:, :])
                        self.dump("ssq", ssq[:, :])
                        self.dump("mcol", mcol[:, :])
                        self.dump("varc", varc[:, :])
                        self.dump("rcol", rcol[:, :])
                    self.barrier()
                    return
                rtr_done = {}
                evtr_toks = {}
                zt_toks = {}
                tr_toks = {}
                gx_toks = {}
                troh_toks = {}
                evlog2_toks = {}
                for ti in range(NT):
                    par = ti % 2

                    def nrm(v, ti=ti, par=par):
                        self.w("vector", v, *stats_done[ti])
                        if ti == 0:
                            self.w("vector", v, nrm_tok)
                        if ti >= 2:
                            self.w("vector", v, tr_toks[ti - 2])
                        self.rec("vector", v.tensor_scalar(zt[:, par, :], xt[:, ti, :],
                                                           rcol[:, ti:ti + 1], negmr[:, ti:ti + 1],
                                                           ALU.mult, ALU.add))
                    self.emit("vector", nrm)
                    zt_toks[ti] = self.tok("vector")

                    def cst(v, ti=ti, par=par):
                        v.drain()
                        self.rec("vector", v.tensor_copy(zbf[:, ti, :], zt[:, par, :]))
                    self.emit("vector", cst)

                    def trs(teng, ti=ti, par=par):
                        self.w("tensor", teng, zt_toks[ti])
                        if ti >= 2:
                            self.w("tensor", teng, evtr_toks[ti - 2])
                        for c in range(NCH):
                            self.rec("tensor", teng.transpose(
                                ptr[2 * par + c // 4][:, (c % 4) * 128:(c % 4 + 1) * 128],
                                zt[:, par, c * 128:(c + 1) * 128], self.idn[:, :]))
                    self.emit("tensor", trs)
                    tr_toks[ti] = self.tok("tensor")

                    def evtr(v, ti=ti, par=par):
                        self.w("vector", v, tr_toks[ti])
                        self.rec("vector", v.tensor_copy(zfm[:, par, 0:512], ptr[2 * par][:, :]))
                        self.rec("vector", v.tensor_copy(zfm[:, par, 512:1024], ptr[2 * par + 1][:, :]))
                    self.emit("vector", evtr)
                    evtr_toks[ti] = self.tok("vector")

                    def rtr(teng, ti=ti, par=par):
                        self.w("tensor", teng, evtr_toks[ti])
                        for c in range(NCH):
                            self.rec("tensor", teng.matmul(plog[:, :], self.wr_sb[:, c, :],
                                                           zfm[:, par, c * 128:(c + 1) * 128],
                                                           start=(c == 0), stop=(c == NCH - 1)))
                    self.emit("tensor", rtr)
                    mm_tok = self.tok("tensor")

                    def evlog(v, ti=ti, par=par, mm_tok=mm_tok):
                        self.w("vector", v, mm_tok)
                        self.rec("vector", v.tensor_copy(logT[:, par, :], plog[:, :]))
                    self.emit("vector", evlog)
                    lt_tok = self.tok("vector")

                    def trlog(teng, ti=ti, par=par, lt_tok=lt_tok):
                        self.w("tensor", teng, lt_tok)
                        if ti >= 2:
                            self.w("tensor", teng, evlog2_toks[ti - 2])
                        self.rec("tensor", teng.transpose(pl2[par][:, :], logT[:, par, :],
                                                          self.idn[0:E, 0:E]))
                    self.emit("tensor", trlog)
                    tl_tok = self.tok("tensor")

                    def evlog2(v, ti=ti, par=par, tl_tok=tl_tok):
                        self.w("vector", v, tl_tok)
                        self.rec("vector", v.tensor_copy(logits[:, ti, :], pl2[par][:, :]))
                    self.emit("vector", evlog2)
                    evlog2_toks[ti] = self.tok("vector")

                    def amx(v, ti=ti):
                        if ti >= 1:
                            self.w("vector", v, gx_toks[ti - 1])
                        v.drain()
                        self.rec("vector", v.max(mx8[:, :], logits[:, ti, :]))
                        v.drain()
                        self.rec("vector", v.tensor_scalar(nmx[:, 0:1], mx8[:, 0:1], -1.0, None, ALU.mult))
                        self.rec("vector", v.tensor_scalar(eqf[:, :], logits[:, ti, :], mx8[:, 0:1],
                                                           None, ALU.is_equal))
                        v.drain()
                        self.rec("vector", v.tensor_tensor(cand[:, :], eqf[:, :], self.iotaEm[:, :], ALU.mult))
                        v.drain()
                        self.rec("vector", v.tensor_reduce(mn[:, :], cand[:, :], AX, ALU.min))
                        v.drain()
                        self.rec("vector", v.tensor_scalar(top1c[:, ti:ti + 1], mn[:, :], float(BIG), None, ALU.add))
                        v.drain()
                    self.emit("vector", amx)
                    am_tok = self.tok("vector")

                    def gx(s, ti=ti, am_tok=am_tok):
                        self.w("scalar", s, am_tok)
                        if ti >= 1:
                            self.w("scalar", s, gx_toks[ti - 1])
                        self.rec("scalar", s.activation(expsc[:, :], logits[:, ti, :], AF.Exp,
                                                        bias=nmx[:, 0:1], accum_out=den[:, ti:ti + 1]))
                    self.emit("scalar", gx)
                    gx_toks[ti] = self.tok("scalar")

                    def gr(v, ti=ti):
                        self.w("vector", v, gx_toks[ti])
                        if ti >= 1:
                            self.w("vector", v, troh_toks[ti - 1])
                        self.rec("vector", v.reciprocal(gatec[:, ti:ti + 1], den[:, ti:ti + 1]))
                        self.rec("vector", v.tensor_scalar(ohtm[:, :], self.iotaE[:, :], top1c[:, ti:ti + 1],
                                                           None, ALU.is_equal))
                    self.emit("vector", gr)
                    oh_tok = self.tok("vector")

                    def troh(teng, ti=ti, oh_tok=oh_tok):
                        self.w("tensor", teng, oh_tok)
                        self.rec("tensor", teng.transpose(plog[:, :], ohtm[:, :], self.idn[:, :]))
                    self.emit("tensor", troh)
                    troh_toks[ti] = self.tok("tensor")

                    def evoh(v, ti=ti):
                        self.w("vector", v, troh_toks[ti])
                        self.rec("vector", v.tensor_copy(oh_em[:, ti * 128:(ti + 1) * 128], plog[:, :]))
                    self.emit("vector", evoh)

                def scan(v):
                    v.drain()
                    self.rec("vector", v.tensor_tensor_scan(cums[:, :], oh_em[:, :], oh_em[:, :],
                                                            0.0, ALU.add, ALU.bypass))
                self.emit("vector", scan)
                scan_tok = self.tok("vector")

                def gcnt(g, scan_tok=scan_tok):
                    self.w("gpsimd", g, scan_tok)
                    wtok = self.dtok("MISC", g.dma_start(
                        out=cnt_in.ap().rearrange("(p one) -> p one", one=1),
                        in_=cums[:, S - 1:S]))
                    self.w("gpsimd", g, wtok)
                    self.ag1_tok = self.cctok(g.collective_compute(
                        "AllGather", ALU.bypass, replica_groups=[list(range(NCORE))],
                        ins=[cnt_in.ap().opt()], outs=[cnt_out.ap().opt()]))
                self.emit("gpsimd", gcnt)

                pfxo = tt("pfxo", [E, 1])
                pfxrow = tt("pfxrow", [1, E])
                pfxbc = tt("pfxbc", [128, E])

                def pfx(teng, scan_tok=scan_tok):
                    self.w("tensor", teng, scan_tok)
                    self.rec("tensor", teng.matmul(pmisc[0:E, 0:1], self.Lsb[:, :], cums[:, S - 1:S],
                                                   start=True, stop=True))
                self.emit("tensor", pfx)
                pfx_tok = self.tok("tensor")

                def evpfx(v, pfx_tok=pfx_tok):
                    self.w("vector", v, pfx_tok)
                    self.rec("vector", v.tensor_copy(pfxo[:, :], pmisc[0:E, 0:1]))
                self.emit("vector", evpfx)
                evp_tok = self.tok("vector")

                def trpfx(teng, evp_tok=evp_tok):
                    self.w("tensor", teng, evp_tok)
                    self.rec("tensor", teng.transpose(pmisc[0:1, 32:32 + E], pfxo[:, :], self.idn[0:E, 0:E]))
                self.emit("tensor", trpfx)
                trp_tok = self.tok("tensor")

                def evrow(v, trp_tok=trp_tok):
                    self.w("vector", v, trp_tok)
                    self.rec("vector", v.tensor_copy(pfxrow[0:1, :], pmisc[0:1, 32:32 + E]))
                self.emit("vector", evrow)
                evr_tok = self.tok("vector")

                def bcpfx(teng, evr_tok=evr_tok):
                    self.w("tensor", teng, evr_tok)
                    self.rec("tensor", teng.matmul(pmisc[:, 64:64 + E], self.ones1[0:1, :], pfxrow[0:1, :],
                                                   start=True, stop=True))
                self.emit("tensor", bcpfx)
                bcp_tok = self.tok("tensor")

                def evbc(v, bcp_tok=bcp_tok):
                    self.w("vector", v, bcp_tok)
                    self.rec("vector", v.tensor_copy(pfxbc[:, :], pmisc[:, 64:64 + E]))
                self.emit("vector", evbc)
                pfxbc_tok = self.tok("vector")

                if self._ffn_stop == "f2a":
                    self.barrier()
                    if inv == 0:
                        self.dump("pfxbc", pfxbc[:, :])
                        self.dump("cums0", cums[:, 0:128])
                    self.barrier()
                    return
                slotL = tt("slotL", [128, NT], I32)
                ctm = tt("ctm", [128, E])
                sltf = tt("sltf", [128, 1])
                ctm8 = tt("ctm8", [128, E])
                oht8 = tt("oht8", [128, E])
                ctm_done = {}
                for ti in range(NT):
                    def trc(teng, ti=ti, pfxbc_tok=pfxbc_tok):
                        self.w("tensor", teng, pfxbc_tok)
                        if ti >= 2:
                            self.w("tensor", teng, ctm_done[ti - 2])
                        self.rec("tensor", teng.transpose(pl2[ti % 2][:, :], cums[:, ti * 128:(ti + 1) * 128],
                                                          self.idn[0:E, 0:E]))
                    self.emit("tensor", trc)
                    trc_tok = self.tok("tensor")

                    def vslot_a(v, ti=ti, trc_tok=trc_tok):
                        self.w("vector", v, trc_tok)
                        self.rec("vector", v.tensor_copy(ctm[:, :], pl2[ti % 2][:, :]))
                    self.emit("vector", vslot_a)
                    ctm_done[ti] = self.tok("vector")

                    def vslot(v, ti=ti):
                        self.rec("vector", v.tensor_scalar(ohtm[:, :], self.iotaE[:, :], top1c[:, ti:ti + 1],
                                                           None, ALU.is_equal))
                        v.drain()
                        if ti == 8:
                            self.rec("vector", v.tensor_copy(ctm8[:, :], ctm[:, :]))
                            self.rec("vector", v.tensor_copy(oht8[:, :], ohtm[:, :]))
                            v.drain()
                        self.rec("vector", v.tensor_tensor(prod[:, :], ohtm[:, :], ctm[:, :], ALU.mult))
                        v.drain()
                        self.rec("vector", v.reduce_sum(poslc[:, ti:ti + 1], prod[:, :], AX))
                        v.drain()
                        self.rec("vector", v.tensor_scalar(poslc[:, ti:ti + 1], poslc[:, ti:ti + 1], -1.0,
                                                           None, ALU.add))
                        self.rec("vector", v.tensor_tensor(prod[:, :], ohtm[:, :], pfxbc[:, :], ALU.mult))
                        v.drain()
                        self.rec("vector", v.reduce_sum(sltf[:, :], prod[:, :], AX))
                        v.drain()
                        self.rec("vector", v.tensor_tensor(sltf[:, :], sltf[:, :], poslc[:, ti:ti + 1],
                                                           ALU.add))
                        v.drain()
                        self.rec("vector", v.tensor_scalar(sltf[:, :], sltf[:, :], float(S - 1), 0.0,
                                                           ALU.min, ALU.max))
                        v.drain()
                        self.rec("vector", v.tensor_copy(slotL[:, ti:ti + 1], sltf[:, :]))
                    self.emit("vector", vslot)
                    vs_tok = self.tok("vector")
                    if self._ffn_stop == "f2t":
                        self.barrier()

                    if self._ffn_stop != "f2t":
                        def sct(g, ti=ti, vs_tok=vs_tok):
                            self.w("gpsimd", g, vs_tok)
                            self.dtok("SCT", g.indirect_dma_start(
                                out=dsp_in[:, :],
                                out_offset=IndirectOffsetOnAxis(slotL[:, ti:ti + 1], 0),
                                in_=zbf[:, ti, :], in_offset=None))
                        self.emit("gpsimd", sct)

                if self._ffn_stop in ("f2t", "f2s"):
                    self.barrier()
                    if inv == 0:
                        self.dump("slotL", slotL[:, :])
                        self.dump("posl", poslc[:, :])
                        self.dump("cums1", cums[:, 128:256])
                        self.dump("pfxbc", pfxbc[:, :])
                        self.dump("ctm8", ctm8[:, :])
                        self.dump("oht8", oht8[:, :])
                    self.barrier()
                    return

                def ag2(g):
                    self.w("gpsimd", g, ("SCT", self.cnt.get("SCT", 0)))
                    self.ag2_tok = self.cctok(g.collective_compute(
                        "AllGather", ALU.bypass, replica_groups=[list(range(NCORE))],
                        ins=[dsp_in.ap().opt()], outs=[dsp_out.ap().opt()]))
                self.emit("gpsimd", ag2)

                if self._ffn_stop == "f2b":
                    self.barrier()
                    if inv == 0:
                        self.dump("slotL", slotL[:, :])
                        self.dump("posl", poslc[:, :])
                    self.barrier()
                    return
                cfm = tt("cfm", [E, NCORE])
                basefm = tt("basefm", [E, NCORE])
                pfxfm = tt("pfxfm", [E, NCORE])
                rows = tt("rows", [1, 160])
                pmb = tt("pmb", [128, NCORE])
                bocol = tt("bocol", [E, 1])

                def ldcnt(eng):
                    self.w("sync", eng, self.ag1_tok)
                    self.cfm_tok = self.dtok("CNT", eng.dma_start(
                        out=cfm[:, :], in_=cnt_out.ap().rearrange("(j e) -> e j", e=E),
                        allow_slow_non_contiguous=True))
                self.emit("sync", ldcnt)

                def vbase(v):
                    self.w("vector", v, self.cfm_tok)
                    self.rec("vector", v.tensor_tensor_scan(basefm[:, :], cfm[:, :], cfm[:, :],
                                                            0.0, ALU.add, ALU.bypass))
                    v.drain()
                    self.rec("vector", v.tensor_tensor(basefm[:, :], basefm[:, :], cfm[:, :], ALU.subtract))
                self.emit("vector", vbase)
                vb_tok = self.tok("vector")

                def tpfxfm(teng, vb_tok=vb_tok):
                    self.w("tensor", teng, vb_tok)
                    self.rec("tensor", teng.matmul(pmisc[0:E, 128:128 + NCORE], self.Lsb[:, :], cfm[:, :],
                                                   start=True, stop=True))
                self.emit("tensor", tpfxfm)
                tp_tok = self.tok("tensor")

                def evpfxfm(v, tp_tok=tp_tok):
                    self.w("vector", v, tp_tok)
                    self.rec("vector", v.tensor_copy(pfxfm[:, :], pmisc[0:E, 128:128 + NCORE]))
                self.emit("vector", evpfxfm)
                ev2_tok = self.tok("vector")

                def myrows(teng, ev2_tok=ev2_tok):
                    self.w("tensor", teng, ev2_tok)
                    for ei in range(EPC):
                        self.rec("tensor", teng.matmul(pmisc[0:1, 160 + ei * 8:168 + ei * 8],
                                                       self.eselsb[:, ei:ei + 1], basefm[:, :],
                                                       start=True, stop=True))
                        self.rec("tensor", teng.matmul(pmisc[0:1, 192 + ei * 8:200 + ei * 8],
                                                       self.eselsb[:, ei:ei + 1], pfxfm[:, :],
                                                       start=True, stop=True))
                self.emit("tensor", myrows)
                mr_tok = self.tok("tensor")

                def evrows(v, mr_tok=mr_tok):
                    self.w("vector", v, mr_tok)
                    self.rec("vector", v.tensor_copy(rows[0:1, 0:32], pmisc[0:1, 160:192]))
                    self.rec("vector", v.tensor_copy(rows[0:1, 32:64], pmisc[0:1, 192:224]))
                self.emit("vector", evrows)
                evrw_tok = self.tok("vector")

                def iota2048(g, evrw_tok=evrw_tok):
                    self.w("gpsimd", g, evrw_tok)
                    self.rec("gpsimd", g.iota(rows[0:1, 64:96], [[0, EPC], [S, NCORE]], base=0,
                                              channel_multiplier=0,
                                              allow_small_or_imprecise_dtypes=True))
                self.emit("gpsimd", iota2048)
                io_tok = self.tok("gpsimd")

                def vcrow(v, io_tok=io_tok):
                    self.w("vector", v, io_tok)
                    self.rec("vector", v.tensor_tensor(rows[0:1, 64:96], rows[0:1, 64:96],
                                                       rows[0:1, 32:64], ALU.add))
                    v.drain()
                    self.rec("vector", v.tensor_tensor(rows[0:1, 64:96], rows[0:1, 64:96],
                                                       rows[0:1, 0:32], ALU.subtract))
                    v.drain()
                    self.rec("vector", v.tensor_tensor(rows[0:1, 97:128], rows[0:1, 65:96],
                                                       rows[0:1, 64:95], ALU.subtract))
                    v.drain()
                    for ei in range(EPC):
                        self.rec("vector", v.tensor_copy(rows[0:1, 96 + ei * 8:97 + ei * 8],
                                                         rows[0:1, 64 + ei * 8:65 + ei * 8]))
                self.emit("vector", vcrow)
                vcr_tok = self.tok("vector")

                def bco(teng, vcr_tok=vcr_tok):
                    self.w("tensor", teng, vcr_tok)
                    self.rec("tensor", teng.matmul(pmisc[:, 224:224 + NCORE], self.ones1[0:1, :],
                                                   self.pmrow[0:1, :], start=True, stop=True))
                    self.rec("tensor", teng.matmul(pmisc[:, 232:232 + E], self.ones1[0:1, :],
                                                   rows[0:1, 0:32], start=True, stop=True))
                    self.rec("tensor", teng.matmul(pmisc[:, 264:264 + E], self.ones1[0:1, :],
                                                   rows[0:1, 96:128], start=True, stop=True))
                self.emit("tensor", bco)
                bco_tok = self.tok("tensor")

                def evbco(v, bco_tok=bco_tok):
                    self.w("vector", v, bco_tok)
                    self.rec("vector", v.tensor_copy(pmb[:, :], pmisc[:, 224:224 + NCORE]))
                    self.rec("vector", v.tensor_copy(basebc[:, :], pmisc[:, 232:232 + E]))
                    self.rec("vector", v.tensor_copy(dltbc[:, :], pmisc[:, 264:264 + E]))
                    v.drain()
                    self.rec("vector", v.tensor_tensor(pmb[0:E, :], cfm[:, :], pmb[0:E, :], ALU.mult))
                    v.drain()
                    self.rec("vector", v.reduce_sum(bocol[:, :], pmb[0:E, :], AX))
                self.emit("vector", evbco)
                bo_tok = self.tok("vector")

                def trbo(teng, bo_tok=bo_tok):
                    self.w("tensor", teng, bo_tok)
                    self.rec("tensor", teng.transpose(pmisc[0:1, 296:296 + E], bocol[:, :],
                                                      self.idn[0:E, 0:E]))
                self.emit("tensor", trbo)
                trbo_tok = self.tok("tensor")

                def evbo2(v, trbo_tok=trbo_tok):
                    self.w("vector", v, trbo_tok)
                    self.rec("vector", v.tensor_copy(rows[0:1, 128:160], pmisc[0:1, 296:296 + E]))
                self.emit("vector", evbo2)
                evbo2_tok = self.tok("vector")

                def bcbo(teng, evbo2_tok=evbo2_tok):
                    self.w("tensor", teng, evbo2_tok)
                    self.rec("tensor", teng.matmul(pmisc[:, 328:328 + E], self.ones1[0:1, :],
                                                   rows[0:1, 128:160], start=True, stop=True))
                self.emit("tensor", bcbo)
                bcbo_tok = self.tok("tensor")

                def evbo3(v, bcbo_tok=bcbo_tok):
                    self.w("vector", v, bcbo_tok)
                    self.rec("vector", v.tensor_copy(baseownbc[:, :], pmisc[:, 328:328 + E]))
                self.emit("vector", evbo3)
                bobc_tok = self.tok("vector")

                retf = tt("retf", [128, 1])
                acc = tt("acc", [128, 1])
                f4c = tt("f4c", [128, 1]); f4i = tt("f4i", [128, 1], I32)
                m2c = tt("m2c", [128, 1]); m2i = tt("m2i", [128, 1], I32)
                for ti in range(NT):
                    def vret(v, ti=ti, bobc_tok=bobc_tok):
                        self.w("vector", v, bobc_tok)
                        self.rec("vector", v.tensor_scalar(ohtm[:, :], self.iotaE[:, :], top1c[:, ti:ti + 1],
                                                           None, ALU.is_equal))
                        v.drain()
                        self.rec("vector", v.tensor_tensor(prod[:, :], ohtm[:, :], baseownbc[:, :], ALU.mult))
                        v.drain()
                        self.rec("vector", v.reduce_sum(retf[:, :], prod[:, :], AX))
                        v.drain()
                        self.rec("vector", v.tensor_tensor(retf[:, :], retf[:, :], poslc[:, ti:ti + 1],
                                                           ALU.add))
                        v.drain()
                        self.rec("vector", v.tensor_scalar(acc[:, :], retf[:, :], float(C), None, ALU.is_lt))
                        v.drain()
                        self.rec("vector", v.tensor_tensor(geff[:, ti:ti + 1], gatec[:, ti:ti + 1],
                                                           acc[:, :], ALU.mult))
                        self.rec("vector", v.tensor_scalar(f4c[:, :], top1c[:, ti:ti + 1], 0.25, -0.49,
                                                           ALU.mult, ALU.add))
                        v.drain()
                        self.rec("vector", v.tensor_copy(f4i[:, :], f4c[:, :]))
                        v.drain()
                        self.rec("vector", v.tensor_copy(f4c[:, :], f4i[:, :]))
                        self.rec("vector", v.tensor_scalar(m2c[:, :], top1c[:, ti:ti + 1], 0.5, -0.49,
                                                           ALU.mult, ALU.add))
                        v.drain()
                        self.rec("vector", v.tensor_copy(m2i[:, :], m2c[:, :]))
                        v.drain()
                        self.rec("vector", v.tensor_copy(m2c[:, :], m2i[:, :]))
                        v.drain()
                        self.rec("vector", v.scalar_tensor_tensor(m2c[:, :], m2c[:, :], -2.0,
                                                                  top1c[:, ti:ti + 1], ALU.mult, ALU.add))
                        v.drain()
                        self.rec("vector", v.scalar_tensor_tensor(retf[:, :], m2c[:, :], float(C),
                                                                  retf[:, :], ALU.mult, ALU.add))
                        v.drain()
                        self.rec("vector", v.scalar_tensor_tensor(retf[:, :], f4c[:, :], float(2 * C),
                                                                  retf[:, :], ALU.mult, ALU.add))
                        v.drain()
                        self.rec("vector", v.tensor_scalar(retf[:, :], retf[:, :], float(NCORE * 2 * C - 1),
                                                           0.0, ALU.min, ALU.max))
                        v.drain()
                        self.rec("vector", v.tensor_copy(reti[:, ti:ti + 1], retf[:, :]))
                        self.rec("vector", v.scalar_tensor_tensor(acc[:, :], f4c[:, :], -4.0,
                                                                  top1c[:, ti:ti + 1], ALU.mult, ALU.add))
                        v.drain()
                        self.rec("vector", v.tensor_scalar(acc[:, :], acc[:, :], 2.0, None, ALU.is_ge))
                        v.drain()
                        self.rec("vector", v.tensor_tensor(geffB[:, ti:ti + 1], geff[:, ti:ti + 1],
                                                           acc[:, :], ALU.mult))
                        v.drain()
                        self.rec("vector", v.tensor_tensor(geff[:, ti:ti + 1], geff[:, ti:ti + 1],
                                                           geffB[:, ti:ti + 1], ALU.subtract))
                    self.emit("vector", vret)
                self.barrier(cc=False)
                if inv == 0:
                    self.dump("ssum", ssum[:, :])
                    self.dump("ssq", ssq[:, :])
                    self.dump("mcol", mcol[:, :])
                    self.dump("varc", varc[:, :])
                    self.dump("rcol", rcol[:, :])
                    self.dump("negmr", negmr[:, :])
                    self.dump("lg", logits[:, :, :])
                    self.dump("top1", top1c[:, :])
                    self.dump("gate", gatec[:, :])
                    self.dump("posl", poslc[:, :])
                    self.dump("geff", geff[:, :])
                    self.dump("slotL", slotL[:, :])
                    self.dump("cfm", cfm[:, :])
                    self.dump("basebc", basebc[:, :])
                    self.dump("dltbc", dltbc[:, :])
                    self.dump("zbf0", zbf[:, 0, :])
                    self.barrier()
                if self._ffn_stop == "f3":
                    return

            with ExitStack() as es:
                def tt(name, shape, dt=F32):
                    return es.enter_context(nc.sbuf_tensor(f"i{inv}b_{name}", shape, dt))
                wgb = tt("wgb", [128, 2, NCH, HD], BF16)
                wub = tt("wub", [128, 2, NCH, HD], BF16)
                wdb = tt("wdb", [128, 2, NCH, HD], BF16)
                buf_tm = tt("buf_tm", [128, 2, CB, HD], BF16)
                buf_fm = tt("buf_fm", [128, NCH, C], BF16)
                h_fm = tt("h_fm", [128, NCH, C], BF16)
                silu_sb = tt("silu_sb", [128, 320], BF16)
                y_sb = tt("y_sb", [128, 2, HD], BF16)
                scol = tt("scol", [128, CB])
                f0 = tt("f0", [128, CB]); f1 = tt("f1", [128, CB])
                gcol = tt("gcol", [128, CB])
                idxc = tt("idxc", [128, EPC, CB], I32)
                pg = [es.enter_context(nc.psum_tensor(f"i{inv}_pg{k}", [128, 320], F32)) for k in range(2)]
                pu = [es.enter_context(nc.psum_tensor(f"i{inv}_pu{k}", [128, 320], F32)) for k in range(2)]
                ptb = es.enter_context(nc.psum_tensor(f"i{inv}_ptb", [128, NCH, 128], BF16))
                py = [es.enter_context(nc.psum_tensor(f"i{inv}_py{k}", [128, 512], F32)) for k in range(2)]

                def sio(g):
                    self.rec("gpsimd", g.iota(scol[:, :], [[128, CB]], base=0, channel_multiplier=1,
                                              allow_small_or_imprecise_dtypes=True))
                self.emit("gpsimd", sio)
                sio_tok = self.tok("gpsimd")

                wtoks = {}

                def wload(g, e):
                    q = "WQ%d" % (e % 2)
                    for (dst, src) in ((wgb, self.wg_in), (wub, self.wu_in), (wdb, self.wd_in)):
                        self.dtok(q, g.dma_start(
                            out=dst[:, e % 2, :, :],
                            in_=src.ap()[e].rearrange("(c p) f -> p c f", p=128)))
                    wtoks[e] = (q, self.cnt[q])

                def wl0(g):
                    wload(g, 0)
                    wload(g, 1)
                self.emit("gpsimd", wl0)

                buf_free = {}
                ydma_par = {}
                vidx_toks = {}
                self.last_slu = None
                self.last_hmul = None
                for e in range(EPC):
                    def vidx(v, e=e):
                        self.w("vector", v, sio_tok)
                        v.drain()
                        self.rec("vector", v.memset(f0[:, :], 0.0))
                        src, dst = f0, f1
                        for j in range(NCORE):
                            col = e * 8 + j
                            v.drain()
                            self.rec("vector", v.tensor_scalar(gcol[:, :], scol[:, :],
                                                               basebc[:, col:col + 1], None, ALU.is_ge))
                            v.drain()
                            self.rec("vector", v.scalar_tensor_tensor(dst[:, :], gcol[:, :],
                                                                      dltbc[:, col:col + 1], src[:, :],
                                                                      ALU.mult, ALU.add))
                            src, dst = dst, src
                        v.drain()
                        self.rec("vector", v.tensor_tensor(src[:, :], src[:, :], scol[:, :], ALU.add))
                        v.drain()
                        self.rec("vector", v.tensor_scalar(src[:, :], src[:, :], float(NCORE * S - 1), 0.0,
                                                           ALU.min, ALU.max))
                        v.drain()
                        self.rec("vector", v.tensor_copy(idxc[:, e, :], src[:, :]))
                    self.emit("vector", vidx)
                    vidx_toks[e] = self.tok("vector")

                ga_full = {}

                def gga(g, e):
                    self.w("gpsimd", g, vidx_toks[e], self.ag2_tok)
                    if e >= 2:
                        self.w("gpsimd", g, buf_free[e - 2])
                    for blk in range(CB):
                        self.dtok("GA%d" % (e % 2), g.indirect_dma_start(
                            out=buf_tm[:, e % 2, blk, :], out_offset=None,
                            in_=dsp_out[:, :],
                            in_offset=IndirectOffsetOnAxis(idxc[:, e, blk:blk + 1], 0)))
                    ga_full[e] = ("GA%d" % (e % 2), self.cnt["GA%d" % (e % 2)])

                def g01(g):
                    gga(g, 0)
                    gga(g, 1)
                self.emit("gpsimd", g01)

                for e in range(EPC):
                    evt_toks = []
                    for blk in range(CB):
                        def trb(teng, e=e, blk=blk):
                            self.w("tensor", teng, ga_full[e])
                            if blk >= 1:
                                self.w("tensor", teng, evt_toks[blk - 1])
                            for c in range(NCH):
                                self.rec("tensor", teng.transpose(ptb[:, c, :],
                                                                  buf_tm[:, e % 2, blk, c * 128:(c + 1) * 128],
                                                                  self.idnb[:, :]))
                        self.emit("tensor", trb)
                        trb_tok = self.tok("tensor")

                        def evb(v, e=e, blk=blk, trb_tok=trb_tok):
                            self.w("vector", v, trb_tok)
                            self.rec("vector", v.tensor_copy(buf_fm[:, :, blk * 128:(blk + 1) * 128],
                                                             ptb[:, :, :]))
                        self.emit("vector", evb)
                        evt_toks.append(self.tok("vector"))
                    buf_free[e] = evt_toks[-1]
                    buf_ready = evt_toks[-1]
                    if e + 2 < EPC:
                        def ggn(g, e=e):
                            gga(g, e + 2)
                        self.emit("gpsimd", ggn)

                    par_toks = [None, None]
                    for itn, (ft, half) in enumerate([(a, b) for a in range(NCH) for b in range(2)]):
                        par = itn % 2

                        def gmm(teng, e=e, ft=ft, half=half, par=par, buf_ready=buf_ready):
                            self.w("tensor", teng, buf_ready, wtoks[e])
                            if par_toks[par] is not None:
                                self.w("tensor", teng, par_toks[par])
                            for c in range(NCH):
                                self.rec("tensor", teng.matmul(
                                    pg[par][:, :], wgb[:, e % 2, c, ft * 128:(ft + 1) * 128],
                                    buf_fm[:, c, half * 320:(half + 1) * 320],
                                    start=(c == 0), stop=(c == NCH - 1)))
                            for c in range(NCH):
                                self.rec("tensor", teng.matmul(
                                    pu[par][:, :], wub[:, e % 2, c, ft * 128:(ft + 1) * 128],
                                    buf_fm[:, c, half * 320:(half + 1) * 320],
                                    start=(c == 0), stop=(c == NCH - 1)))
                        self.emit("tensor", gmm)
                        gmm_tok = self.tok("tensor")

                        def slu(s, par=par, gmm_tok=gmm_tok, prev=None):
                            self.w("scalar", s, gmm_tok)
                            if self.last_slu is not None:
                                self.w("scalar", s, self.last_slu)
                            if self.last_hmul is not None:
                                self.w("scalar", s, self.last_hmul)
                            self.rec("scalar", s.activation(silu_sb[:, :], pg[par][:, :], AF.Silu))
                        self.emit("scalar", slu)
                        slu_tok = self.tok("scalar")
                        self.last_slu = slu_tok

                        def hmul(v, ft=ft, half=half, par=par, slu_tok=slu_tok):
                            self.w("vector", v, slu_tok)
                            self.rec("vector", v.tensor_tensor(
                                h_fm[:, ft, half * 320:(half + 1) * 320],
                                silu_sb[:, :], pu[par][:, :], ALU.mult))
                        self.emit("vector", hmul)
                        par_toks[par] = self.tok("vector")
                        self.last_hmul = par_toks[par]
                    h_ready = par_toks[1]
                    h_ready2 = par_toks[0]

                    yev_toks = [None, None]
                    last_yev = None
                    for tt_i in range(CB):
                        for half in range(2):
                            def ymm(teng, e=e, tt_i=tt_i, half=half, h_ready=h_ready, h_ready2=h_ready2):
                                self.w("tensor", teng, h_ready, h_ready2)
                                if yev_toks[half] is not None:
                                    self.w("tensor", teng, yev_toks[half])
                                for f in range(NCH):
                                    self.rec("tensor", teng.matmul(
                                        py[half][:, :], h_fm[:, f, tt_i * 128:(tt_i + 1) * 128],
                                        wdb[:, e % 2, f, half * 512:(half + 1) * 512],
                                        start=(f == 0), stop=(f == NCH - 1)))
                            self.emit("tensor", ymm)
                            ymm_tok = self.tok("tensor")

                            def yev(v, tt_i=tt_i, half=half, ymm_tok=ymm_tok):
                                self.w("vector", v, ymm_tok)
                                if ydma_par.get(tt_i % 2) is not None:
                                    self.w("vector", v, ydma_par[tt_i % 2])
                                self.rec("vector", v.tensor_copy(
                                    y_sb[:, tt_i % 2, half * 512:(half + 1) * 512], py[half][:, :]))
                            self.emit("vector", yev)
                            yev_toks[half] = self.tok("vector")
                        last_yev = yev_toks[1]

                        def yst(eng, e=e, tt_i=tt_i, yv=yev_toks[1]):
                            self.w("sync", eng, yv)
                            ydma_par[tt_i % 2] = self.dtok("YST%d" % (tt_i % 2), eng.dma_start(
                                out=y_in[e // 2][((e % 2) * CB + tt_i) * 128:
                                                 ((e % 2) * CB + tt_i + 1) * 128, :],
                                in_=y_sb[:, tt_i % 2, :]))
                        self.emit("sync", yst)

                    if e == 1:
                        def ag3a(g):
                            self.w("gpsimd", g, ("YST0", self.cnt.get("YST0", 0)),
                                   ("YST1", self.cnt.get("YST1", 0)))
                            self.ag3a_tok = self.cctok(g.collective_compute(
                                "AllGather", ALU.bypass, replica_groups=[list(range(NCORE))],
                                ins=[y_in[0].ap().opt()], outs=[y_out[0].ap().opt()]))
                        self.emit("gpsimd", ag3a)

                    if e + 2 < EPC:
                        def wln(g, e=e, last_yev=last_yev):
                            self.w("gpsimd", g, last_yev)
                            wload(g, e + 2)
                        self.emit("gpsimd", wln)

                if inv == 0 and "yin" in self._dumps:
                    dyt = self.nc.dram_tensor("dbg_yin", [EPC * C, HD], BF16, kind="ExternalOutput")

                    def ycp(eng):
                        self.w("sync", eng, ("YST0", self.cnt.get("YST0", 0)),
                               ("YST1", self.cnt.get("YST1", 0)))
                        self.dtok("ST0", eng.dma_start(out=dyt[:, :], in_=y_in[0][:, :]))
                    self.emit("sync", ycp)
                    dbt = self.nc.dram_tensor("dbg_buffm", [128, NCH * C], BF16, kind="ExternalOutput")

                    def bcp(eng):
                        self.dtok("ST0", eng.dma_start(out=dbt[:, :], in_=buf_fm[:, :, :]))
                    self.emit("sync", bcp)

                def ag3(g):
                    self.w("gpsimd", g, ("YST0", self.cnt.get("YST0", 0)),
                           ("YST1", self.cnt.get("YST1", 0)))
                    self.ag3_tok = self.cctok(g.collective_compute(
                        "AllGather", ALU.bypass, replica_groups=[list(range(NCORE))],
                        ins=[y_in[1].ap().opt()], outs=[y_out[1].ap().opt()]))
                self.emit("gpsimd", ag3)
                self.barrier(cc=False)
                if inv == 0:
                    self.dump("idxc", idxc[:, :])
                    self.barrier()
                if self._ffn_stop == "f4":
                    return

            with ExitStack() as es:
                def tt(name, shape, dt=F32):
                    return es.enter_context(nc.sbuf_tensor(f"i{inv}c_{name}", shape, dt))
                ytokA = tt("ytokA", [128, 2, HD])
                ytokB = tt("ytokB", [128, 2, HD])
                xt = tt("xt", [128, NT, HD])
                zt = tt("zt", [128, 2, HD])
                cmb = tt("cmb", [128, 4, HD])
                ga = {}
                ld = {}
                done = {}
                st_toks = {}
                for ti in range(NT):
                    par = ti % 2

                    def gga(g, ti=ti, par=par):
                        self.w("gpsimd", g, self.ag3a_tok, self.ag3_tok)
                        if ti >= 2:
                            self.w("gpsimd", g, done[ti - 2])
                        self.dtok("GA%d" % par, g.indirect_dma_start(
                            out=ytokA[:, par, :], out_offset=None,
                            in_=y_out[0][:, :],
                            in_offset=IndirectOffsetOnAxis(reti[:, ti:ti + 1], 0)))
                        ga[ti] = self.dtok("GA%d" % par, g.indirect_dma_start(
                            out=ytokB[:, par, :], out_offset=None,
                            in_=y_out[1][:, :],
                            in_offset=IndirectOffsetOnAxis(reti[:, ti:ti + 1], 0)))
                    self.emit("gpsimd", gga)

                    def ldx(eng, ti=ti, par=par):
                        if ti >= 2:
                            self.w("sync", eng, done[ti - 2])
                        ld[ti] = self.dtok("LD%d" % par, eng.dma_start(
                            out=xt[:, par, :], in_=xsrc[ti * 128:(ti + 1) * 128, :]))
                    self.emit("sync", ldx)

                    def znrm(v, ti=ti, par=par):
                        self.w("vector", v, ld[ti])
                        if ti >= 2:
                            self.w("vector", v, done[ti - 2])
                        self.rec("vector", v.tensor_scalar(zt[:, par, :], xt[:, par, :],
                                                           rcol[:, ti:ti + 1], negmr[:, ti:ti + 1],
                                                           ALU.mult, ALU.add))
                    self.emit("vector", znrm)
                    z_tok = self.tok("vector")

                    def vcmb(v, ti=ti, par=par, z_tok=z_tok):
                        self.w("vector", v, z_tok, ga[ti])
                        if ti >= 4:
                            self.w("vector", v, st_toks[ti - 4])
                        self.rec("vector", v.scalar_tensor_tensor(cmb[:, ti % 4, :], ytokA[:, par, :],
                                                                  geff[:, ti:ti + 1], zt[:, par, :],
                                                                  ALU.mult, ALU.add))
                        v.drain()
                        self.rec("vector", v.scalar_tensor_tensor(cmb[:, ti % 4, :], ytokB[:, par, :],
                                                                  geffB[:, ti:ti + 1], cmb[:, ti % 4, :],
                                                                  ALU.mult, ALU.add))
                        v.drain()
                        self.rec("vector", v.scalar_tensor_tensor(cmb[:, ti % 4, :], cmb[:, ti % 4, :], 0.5,
                                                                  xt[:, par, :], ALU.mult, ALU.add))
                        if stats_out is not None:
                            v.drain()
                            self.rec("vector", v.reduce_sum(self.hs_sum[stats_out][:, ti:ti + 1],
                                                            cmb[:, ti % 4, :], AX))
                    self.emit("vector", vcmb)
                    c_tok = self.tok("vector")

                    if stats_out is not None:
                        def sstf(s, ti=ti, par=par, c_tok=c_tok):
                            self.w("scalar", s, c_tok)
                            self.rec("scalar", s.activation(zt[:, par, :], cmb[:, ti % 4, :], AF.Square,
                                                            accum_out=self.hs_ssq[stats_out][:, ti:ti + 1]))
                        self.emit("scalar", sstf)
                        done[ti] = (self.tok("scalar"))
                    else:
                        done[ti] = c_tok

                    def stx(eng, ti=ti, par=par, c_tok=c_tok):
                        self.w("sync", eng, c_tok)
                        st_toks[ti] = self.dtok("ST%d" % (ti % 4), eng.dma_start(
                            out=xdst[ti * 128:(ti + 1) * 128, :], in_=cmb[:, ti % 4, :]))
                    self.emit("sync", stx)
                if stats_out is not None:
                    vt2 = self.tok("vector")
                    st2 = self.tok("scalar")
                    self.hs_tok[stats_out] = st2 if st2 is not None else vt2
                self.barrier()

    def attn(self, stats_in=None, stats_out=None):
        nc = self.nc
        with ExitStack() as att:
            def at(name, shape, dt=F32):
                return att.enter_context(nc.sbuf_tensor(f"at_{name}", shape, dt))
            qfm = at("qfm", [128, NCH, S], BF16)
            kfm = at("kfm", [128, NCH, S], BF16)
            vtm = at("vtm", [128, NT, HD], BF16)
            masks = at("masks", [128, 5, 512])
            biasv = at("biasv", [128, HD])
            bvrow = at("bvrow", [1, HD])
            bqkvT = at("bqkvT", [128, 24])

            with ExitStack() as es:
                def tt(name, shape, dt=F32):
                    return es.enter_context(nc.sbuf_tensor(f"a1_{name}", shape, dt))
                xt = tt("xt", [128, 4, HD])
                z2b = tt("z2b", [128, 2, HD], BF16)
                z2fm = tt("z2fm", [128, NCH, S], BF16)
                ssum = tt("ssum", [128, NT]); ssq = tt("ssq", [128, NT])
                sqs = tt("sqs", [128, HD])
                mcol = tt("mcol", [128, NT]); rcol = tt("rcol", [128, NT]); negmr = tt("negmr", [128, NT])
                ptb = [es.enter_context(nc.psum_tensor(f"a1_ptb{k}", [128, NCH, 128], BF16)) for k in range(2)]

                ldt = {}
                sdone = {}
                if stats_in is not None:
                    ssum = self.hs_sum[stats_in]
                    ssq = self.hs_ssq[stats_in]
                for ti in (() if stats_in is not None else range(NT)):
                    par = ti % 2

                    def ldx(eng, ti=ti, par=par):
                        if ti >= 2:
                            self.w("sync", eng, *sdone[ti - 2])
                        ldt[ti] = self.dtok("LD%d" % par, eng.dma_start(
                            out=xt[:, par, :], in_=self.xa[ti * 128:(ti + 1) * 128, :]))
                    self.emit("sync", ldx)

                    def sst(s, ti=ti, par=par):
                        self.w("scalar", s, ldt[ti])
                        if ti >= 1:
                            self.w("scalar", s, sdone[ti - 1][0])
                        self.rec("scalar", s.activation(sqs[:, :], xt[:, par, :], AF.Square,
                                                        accum_out=ssq[:, ti:ti + 1]))
                    self.emit("scalar", sst)
                    s_tok = self.tok("scalar")

                    def vst(v, ti=ti, par=par):
                        self.w("vector", v, ldt[ti])
                        self.rec("vector", v.reduce_sum(ssum[:, ti:ti + 1], xt[:, par, :], AX))
                    self.emit("vector", vst)
                    sdone[ti] = (s_tok, self.tok("vector"))

                def vc(v):
                    if stats_in is None:
                        self.w("vector", v, *sdone[NT - 1])
                    v.drain()
                    self.rec("vector", v.tensor_scalar(mcol[:, :], ssum[:, :], 1.0 / HD, None, ALU.mult))
                    v.drain()
                    self.rec("vector", v.tensor_tensor(rcol[:, :], mcol[:, :], mcol[:, :], ALU.mult))
                    v.drain()
                    self.rec("vector", v.tensor_scalar(rcol[:, :], rcol[:, :], -1.0, None, ALU.mult))
                    v.drain()
                    self.rec("vector", v.scalar_tensor_tensor(rcol[:, :], ssq[:, :], 1.0 / HD, rcol[:, :],
                                                              ALU.mult, ALU.add))
                self.emit("vector", vc)
                vt_ = self.tok("vector")

                def sc(s, vt_=vt_):
                    self.w("scalar", s, vt_)
                    self.rec("scalar", s.activation(rcol[:, :], rcol[:, :], AF.Sqrt, bias=self.csts[:, 0:1], scale=1.0))
                self.emit("scalar", sc)
                st_ = self.tok("scalar")

                def vc2(v, st_=st_):
                    self.w("vector", v, st_)
                    self.rec("vector", v.reciprocal(rcol[:, :], rcol[:, :]))
                    v.drain()
                    self.rec("vector", v.scalar_tensor_tensor(negmr[:, :], mcol[:, :], -1.0, rcol[:, :],
                                                              ALU.mult, ALU.mult))
                    v.drain()
                self.emit("vector", vc2)
                nrm_tok = self.tok("vector")

                trdone = {}
                for ti in range(NT):
                    par = ti % 4

                    def ldx2(eng, ti=ti, par=par):
                        if ti >= 4:
                            self.w("sync", eng, trdone[ti - 4])
                        elif stats_in is None and ti >= 2:
                            self.w("sync", eng, *sdone[NT - 2 + ti - 2])
                            self.w("sync", eng, vt_)
                        elif stats_in is None:
                            self.w("sync", eng, *sdone[NT - 4 + ti])
                            self.w("sync", eng, vt_)
                        ldt[ti] = self.dtok("LD%d" % par, eng.dma_start(
                            out=xt[:, par, :], in_=self.xa[ti * 128:(ti + 1) * 128, :]))
                    self.emit("sync", ldx2)

                    def nr(v, ti=ti, par=par):
                        self.w("vector", v, ldt[ti], nrm_tok)
                        if ti >= 2:
                            self.w("vector", v, trdone[ti - 2])
                        self.rec("vector", v.tensor_scalar(z2b[:, par % 2, :], xt[:, par, :],
                                                           rcol[:, ti:ti + 1], negmr[:, ti:ti + 1],
                                                           ALU.mult, ALU.add))
                    self.emit("vector", nr)
                    cb_tok = self.tok("vector")

                    def trz(teng, ti=ti, par=par, cb_tok=cb_tok):
                        self.w("tensor", teng, cb_tok)
                        if ti >= 2:
                            self.w("tensor", teng, trdone[ti - 2])
                        for c in range(NCH):
                            self.rec("tensor", teng.transpose(ptb[ti % 2][:, c, :],
                                                              z2b[:, par % 2, c * 128:(c + 1) * 128],
                                                              self.idnb[:, :]))
                    self.emit("tensor", trz)
                    trz_tok = self.tok("tensor")

                    def evz(v, ti=ti, trz_tok=trz_tok):
                        self.w("vector", v, trz_tok)
                        self.rec("vector", v.tensor_copy(z2fm[:, :, ti * 128:(ti + 1) * 128], ptb[ti % 2][:, :, :]))
                    self.emit("vector", evz)
                    trdone[ti] = self.tok("vector")
                zfm_ready = trdone[NT - 1]

                wqs = tt("wqs", [128, 2, NCH, 512], BF16)
                wvs = tt("wvs", [128, NCH, HD], BF16)
                pqk = [es.enter_context(nc.psum_tensor(f"a2_pqk{k}", [128, 512], F32)) for k in range(4)]
                pv = [es.enter_context(nc.psum_tensor(f"a2_pv{k}", [128, 512], F32)) for k in range(2)]

                def misc_ld(g):
                    self.dtok("MISC", g.dma_start(out=bqkvT[:, :], in_=self.bqkvT_in[:, :]))
                    self.dtok("MISC", g.dma_start(out=bvrow[0:1, :],
                                                  in_=self.bqkv_in.ap()[2 * HD:3 * HD].rearrange(
                                                      "(one f) -> one f", one=1)))
                    self.bias_ld = ("MISC", self.cnt["MISC"])
                self.emit("gpsimd", misc_ld)

                def bvb(teng):
                    self.w("tensor", teng, self.bias_ld)
                    self.rec("tensor", teng.matmul(pv[0][:, :], self.ones1[0:1, :], bvrow[0:1, 0:512],
                                                   start=True, stop=True))
                    self.rec("tensor", teng.matmul(pv[1][:, :], self.ones1[0:1, :], bvrow[0:1, 512:1024],
                                                   start=True, stop=True))
                self.emit("tensor", bvb)
                bvb_tok = self.tok("tensor")

                def evbv(v, bvb_tok=bvb_tok):
                    self.w("vector", v, bvb_tok)
                    self.rec("vector", v.tensor_copy(biasv[:, 0:512], pv[0][:, :]))
                    self.rec("vector", v.tensor_copy(biasv[:, 512:1024], pv[1][:, :]))
                self.emit("vector", evbv)
                bv_tok = self.tok("vector")

                wq_toks = {}

                def wqload(g, sl):
                    q = "WQ%d" % (sl % 2)
                    self.dtok(q, g.dma_start(
                        out=wqs[:, sl % 2, :, :],
                        in_=self.wqkvT.ap()[:, sl * 512:(sl + 1) * 512].rearrange("(c p) o -> p c o",
                                                                                 p=128)))
                    wq_toks[sl] = (q, self.cnt[q])

                def wq0(g):
                    wqload(g, 0)
                    wqload(g, 1)
                self.emit("gpsimd", wq0)

                qk_ev = [None, None, None, None]
                qkn = 0
                for sl in range(4):
                    for o4 in range(4):
                        otile = sl * 4 + o4
                        dst = qfm if otile < 8 else kfm
                        dch = otile % 8
                        for tb in range(4):
                            par = qkn % 4
                            qkn += 1

                            def qmm(teng, sl=sl, o4=o4, tb=tb, par=par, zfm_ready=zfm_ready):
                                self.w("tensor", teng, zfm_ready, wq_toks[sl])
                                if qk_ev[par] is not None:
                                    self.w("tensor", teng, qk_ev[par])
                                for c in range(NCH):
                                    self.rec("tensor", teng.matmul(
                                        pqk[par][:, :], wqs[:, sl % 2, c, o4 * 128:(o4 + 1) * 128],
                                        z2fm[:, c, tb * 512:(tb + 1) * 512],
                                        start=(c == 0), stop=(c == NCH - 1)))
                            self.emit("tensor", qmm)
                            qmm_tok = self.tok("tensor")

                            def qev(s, otile=otile, dch=dch, tb=tb, par=par, dst=dst, qmm_tok=qmm_tok):
                                self.w("scalar", s, qmm_tok)
                                self.rec("scalar", s.activation(dst[:, dch, tb * 512:(tb + 1) * 512],
                                                                pqk[par][:, :], AF.Identity,
                                                                bias=bqkvT[:, otile:otile + 1]))
                            self.emit("scalar", qev)
                            qk_ev[par] = self.tok("scalar")
                    if sl + 2 < 4:
                        def wqn(g, sl=sl, last_q=qmm_tok):
                            self.w("gpsimd", g, last_q)
                            wqload(g, sl + 2)
                        self.emit("gpsimd", wqn)

                def wvl(g):
                    self.dtok("WV", g.dma_start(
                        out=wvs[:, :, :],
                        in_=self.wqkvT.ap()[:, 2 * HD:3 * HD].rearrange("(c p) o -> p c o", p=128)))
                    self.wv_tok = ("WV", self.cnt["WV"])
                self.emit("gpsimd", wvl)

                v_ev = [None, None]
                for tt_i in range(NT):
                    for half in range(2):
                        def vmm(teng, tt_i=tt_i, half=half, zfm_ready=zfm_ready, bv_tok=bv_tok):
                            self.w("tensor", teng, zfm_ready, self.wv_tok)
                            if v_ev[half] is not None:
                                self.w("tensor", teng, v_ev[half])
                            else:
                                self.w("tensor", teng, bv_tok)
                            for c in range(NCH):
                                self.rec("tensor", teng.matmul(
                                    pv[half][:, :], z2fm[:, c, tt_i * 128:(tt_i + 1) * 128],
                                    wvs[:, c, half * 512:(half + 1) * 512],
                                    start=(c == 0), stop=(c == NCH - 1)))
                        self.emit("tensor", vmm)
                        vmm_tok = self.tok("tensor")

                        def vev(v, tt_i=tt_i, half=half, vmm_tok=vmm_tok, bv_tok=bv_tok):
                            self.w("vector", v, vmm_tok, bv_tok)
                            self.rec("vector", v.tensor_tensor(vtm[:, tt_i, half * 512:(half + 1) * 512],
                                                               pv[half][:, :],
                                                               biasv[:, half * 512:(half + 1) * 512],
                                                               ALU.add))
                        self.emit("vector", vev)
                        v_ev[half] = self.tok("vector")
                self.barrier()

            with ExitStack() as es:
                def tt(name, shape, dt=F32):
                    return es.enter_context(nc.sbuf_tensor(f"a3_{name}", shape, dt))
                s_sb = tt("s_sb", [128, NT, 512], BF16)
                xt = tt("xt", [128, 4, HD])
                ot = tt("ot", [128, 4, HD])
                iod = tt("iod", [128, 512])
                iorow = tt("iorow", [128, 512])
                mskf = tt("mskf", [128, 512])
                ps_s = [es.enter_context(nc.psum_tensor(f"a3_ps{k}", [128, 512], F32)) for k in range(4)]
                ps_o = [es.enter_context(nc.psum_tensor(f"a3_po{k}", [128, 512], F32)) for k in range(4)]

                def mgen_g(g):
                    self.rec("gpsimd", g.iota(iorow[:, :], [[1, 512]], base=0, channel_multiplier=0,
                                              allow_small_or_imprecise_dtypes=True))
                self.emit("gpsimd", mgen_g)
                mg_tok = self.tok("gpsimd")

                def mgen_v(v, mg_tok=mg_tok):
                    self.w("vector", v, mg_tok)
                    self.rec("vector", v.tensor_scalar(iod[:, :], iorow[:, :], self.iotc[:, 0:1],
                                                       None, ALU.subtract))
                self.emit("vector", mgen_v)
                mgv_tok = self.tok("vector")

                def mgen_s(s, mgv_tok=mgv_tok):
                    self.w("scalar", s, mgv_tok)
                    self.rec("scalar", s.activation(masks[:, 0, :], iod[:, :], AF.Exp, scale=LNA))
                    self.rec("scalar", s.activation(masks[:, 1, :], iod[:, :], AF.Exp, scale=LNA))
                    for r in range(1, 4):
                        self.rec("scalar", s.activation(masks[:, 1 + r, :], iod[:, :], AF.Exp,
                                                        bias=self.csts[:, r:r + 1], scale=LNA))
                self.emit("scalar", mgen_s)
                ms_tok = self.tok("scalar")

                def mgen_v2(v, ms_tok=ms_tok):
                    self.w("vector", v, ms_tok)
                    for r in range(4):
                        self.rec("vector", v.tensor_scalar(mskf[:, :], iod[:, :], float(128 * r), None,
                                                           ALU.is_ge))
                        v.drain()
                        self.rec("vector", v.tensor_tensor(masks[:, 1 + r, :], masks[:, 1 + r, :],
                                                           mskf[:, :], ALU.mult))
                        v.drain()
                self.emit("vector", mgen_v2)
                msk_tok = self.tok("vector")

                ldx_t = {}
                out_done = {}
                st_toks = {}
                prev_sev = [None, None, None, None]
                prev_oev = [None, None, None, None]
                for qb in range(4):
                    nkt = 4 * qb + 4
                    sev = {}
                    for kt in range(nkt):
                        par = kt % 4

                        def smm(teng, qb=qb, kt=kt, par=par, msk_tok=msk_tok):
                            self.w("tensor", teng, msk_tok)
                            if kt >= 4:
                                self.w("tensor", teng, sev[kt - 4])
                            elif prev_sev[par] is not None:
                                self.w("tensor", teng, prev_sev[par])
                            for c in range(NCH):
                                self.rec("tensor", teng.matmul(
                                    ps_s[par][:, :], kfm[:, c, kt * 128:(kt + 1) * 128],
                                    qfm[:, c, qb * 512:(qb + 1) * 512],
                                    start=(c == 0), stop=(c == NCH - 1)))
                        self.emit("tensor", smm)
                        smm_tok = self.tok("tensor")

                        def sevac(v, qb=qb, kt=kt, par=par, smm_tok=smm_tok):
                            self.w("vector", v, smm_tok)
                            m = 4 * qb - kt
                            if m > 0:
                                mi, scl = 0, SCL * (ALPHA ** (128.0 * m))
                            else:
                                mi, scl = 1 + (-m), SCL
                            self.rec("vector", v.scalar_tensor_tensor(
                                s_sb[:, kt, :], ps_s[par][:, :], scl, masks[:, mi, :],
                                ALU.mult, ALU.mult))
                        self.emit("vector", sevac)
                        sev[kt] = self.tok("vector")
                    for _k in range(4):
                        if nkt - 4 + _k >= 0:
                            prev_sev[(nkt - 4 + _k) % 4] = sev[nkt - 4 + _k]
                    s_ready = sev[nkt - 1]
                    s_ready2 = sev[nkt - 2]

                    for r in range(4):
                        qt = 4 * qb + r
                        par = qt % 2

                        def ldx(eng, qt=qt):
                            if qt >= 4:
                                self.w("sync", eng, out_done[qt - 4])
                            ldx_t[qt] = self.dtok("LD%d" % (qt % 4), eng.dma_start(
                                out=xt[:, qt % 4, :], in_=self.xa[qt * 128:(qt + 1) * 128, :]))
                        self.emit("sync", ldx)

                        oev = {}
                        for half in range(2):
                            ob = (qt * 2 + half) % 4

                            def omm(teng, qt=qt, r=r, half=half, ob=ob, s_ready=s_ready, s_ready2=s_ready2):
                                self.w("tensor", teng, s_ready, s_ready2)
                                if prev_oev[ob] is not None:
                                    self.w("tensor", teng, prev_oev[ob])
                                for kt in range(qt + 1):
                                    self.rec("tensor", teng.matmul(
                                        ps_o[ob][:, :], s_sb[:, kt, r * 128:(r + 1) * 128],
                                        vtm[:, kt, half * 512:(half + 1) * 512],
                                        start=(kt == 0), stop=(kt == qt)))
                            self.emit("tensor", omm)
                            omm_tok = self.tok("tensor")

                            def oevac(v, qt=qt, par=par, half=half, ob=ob, omm_tok=omm_tok):
                                self.w("vector", v, omm_tok, ldx_t[qt])
                                if qt >= 4:
                                    self.w("vector", v, st_toks[qt - 4])
                                self.rec("vector", v.tensor_tensor(
                                    ot[:, qt % 4, half * 512:(half + 1) * 512], ps_o[ob][:, :],
                                    xt[:, qt % 4, half * 512:(half + 1) * 512], ALU.add))
                            self.emit("vector", oevac)
                            oev[half] = self.tok("vector")
                            prev_oev[ob] = oev[half]

                        if stats_out is not None:
                            def vstf(v, qt=qt):
                                v.drain()
                                self.rec("vector", v.reduce_sum(self.hs_sum[stats_out][:, qt:qt + 1],
                                                                ot[:, qt % 4, :], AX))
                            self.emit("vector", vstf)
                            oev[1] = self.tok("vector")

                            def sstf(s, qt=qt, par=par, c_tok=oev[1]):
                                self.w("scalar", s, c_tok)
                                self.rec("scalar", s.activation(xt[:, qt % 4, :], ot[:, qt % 4, :], AF.Square,
                                                                accum_out=self.hs_ssq[stats_out][:, qt:qt + 1]))
                            self.emit("scalar", sstf)
                            out_done[qt] = self.tok("scalar")
                        else:
                            out_done[qt] = oev[1]

                        def stx(eng, qt=qt, c_tok=oev[1]):
                            self.w("sync", eng, c_tok)
                            st_toks[qt] = self.dtok("ST%d" % (qt % 4), eng.dma_start(
                                out=self.xb[qt * 128:(qt + 1) * 128, :], in_=ot[:, qt % 4, :]))
                        self.emit("sync", stx)
                if stats_out is not None:
                    self.hs_tok[stats_out] = self.tok("scalar")
                self.barrier()


def kernel(L0_attn_x, L0_attn_W_qkv, L0_attn_b_qkv, L0_ffn_w_router,
           L0_ffn_w_gate, L0_ffn_w_up, L0_ffn_w_down):
    x = np.ascontiguousarray(np.asarray(L0_attn_x, dtype=np.float32))
    wqkv = np.asarray(L0_attn_W_qkv, dtype=np.float32)
    bqkv = np.ascontiguousarray(np.asarray(L0_attn_b_qkv, dtype=np.float32))
    wrouter = np.ascontiguousarray(np.asarray(L0_ffn_w_router, dtype=np.float32))
    wg = np.asarray(L0_ffn_w_gate, dtype=np.float32)
    wu = np.asarray(L0_ffn_w_up, dtype=np.float32)
    wd = np.asarray(L0_ffn_w_down, dtype=np.float32)

    wqkvT = np.ascontiguousarray(wqkv.T)
    bqkvT = np.ascontiguousarray(bqkv.reshape(24, 128).T)

    kb = KB()
    nc = kb.build()

    in_maps = []
    for i in range(NCORE):
        esel = np.zeros((E, EPC), dtype=np.float32)
        for ei in range(EPC):
            esel[EPC * i + ei, ei] = 1.0
        pmask = np.zeros((NCORE,), dtype=np.float32)
        pmask[:i] = 1.0
        in_maps.append({
            "x": np.ascontiguousarray(x[i]),
            "wqkvT": wqkvT,
            "bqkvT": bqkvT,
            "bqkv": bqkv,
            "wrouter": wrouter,
            "wg": np.ascontiguousarray(wg[EPC * i:EPC * (i + 1)]),
            "wu": np.ascontiguousarray(wu[EPC * i:EPC * (i + 1)]),
            "wd": np.ascontiguousarray(wd[EPC * i:EPC * (i + 1)]),
            "esel": esel,
            "pmask": pmask,
        })
    res = run_bass_kernel_spmd(nc, in_maps, core_ids=list(range(NCORE)))
    out = np.stack([np.asarray(res.results[i]["out"]) for i in range(NCORE)], axis=0)
    return out.astype(np.float32)
